# version 11 (final)
# baseline (speedup 1.0000x reference)
import numpy as np
import ml_dtypes
import concourse.bass as bass
import concourse.mybir as mybir
from concourse.bass_utils import run_bass_kernel_spmd

F32 = mybir.dt.float32
BF16 = mybir.dt.bfloat16
AF = mybir.ActivationFunctionType
ALU = mybir.AluOpType
AX = mybir.AxisListType

H = 2048
G = 16
I_FF = 8192
NCORE = 8

TBL_LN_EXP = 6
TBL_SIGMOID = 2
TBL_SILU = 18


class Buf:
    __slots__ = ("w", "rs")

    def __init__(self):
        self.w = None
        self.rs = []


class Prog:
    ENGS = ("pe", "dve", "act", "pool", "sp")

    def __init__(self, nc):
        self.nc = nc
        self.ops = {e: [] for e in self.ENGS}
        self.chain = {e: nc.alloc_semaphore("ch_" + e) for e in self.ENGS}
        self.cnt = {e: 0 for e in self.ENGS}
        self.streams = {}
        self.water = {}

    def stream(self, name, queue):
        if name not in self.streams:
            self.streams[name] = [self.nc.alloc_semaphore("dm_" + name), 0, queue]

    def _waits(self, eng, deps):
        out = []
        best = {}
        for sem, val in deps:
            if sem is None:
                continue
            key = (eng, sem.name)
            if sem.name == self.chain[eng].name:
                val = min(val, self.cnt[eng])
            if val <= self.water.get(key, -1):
                continue
            if best.get(sem.name, (None, -1))[1] < val:
                best[sem.name] = (sem, val)
        for sem, val in best.values():
            self.water[(eng, sem.name)] = val
            out.append((sem, val))
        return out

    @staticmethod
    def _dep_list(deps, reads, writes, dep_reads=(), dep_writes=()):
        deps = list(deps)
        for b in list(reads) + list(dep_reads):
            if b.w is not None:
                deps.append(b.w)
        for b in list(writes) + list(dep_writes):
            deps.extend(b.rs)
            if b.w is not None:
                deps.append(b.w)
        return deps

    def op(self, eng, fn, deps=(), reads=(), writes=(), dep_reads=(), dep_writes=(), inc=True):
        deps = self._dep_list(deps, reads, writes, dep_reads, dep_writes)
        waits = self._waits(eng, deps)
        sem = self.chain[eng]
        if inc:
            self.cnt[eng] += 1
            tok = (sem, self.cnt[eng])
            self.ops[eng].append((waits, fn, (sem, 1)))
            for b in reads:
                b.rs.append(tok)
            for b in writes:
                b.w = tok
                b.rs = []
            return tok
        self.ops[eng].append((waits, fn, None))
        return None

    def pe_group(self, items, reads=(), writes=(), first_deps=None):
        n = len(items)
        fd = tuple(reads) if first_deps is None else tuple(first_deps)
        for ii, (fn, extra) in enumerate(items):
            if ii == 0:
                if n == 1:
                    return self.op("pe", fn, reads=tuple(reads), dep_reads=fd + tuple(extra),
                                   writes=writes, inc=True)
                self.op("pe", fn, dep_reads=fd + tuple(extra),
                        dep_writes=writes, inc=False)
            elif ii < n - 1:
                self.op("pe", fn, dep_reads=extra, inc=False)
            else:
                return self.op("pe", fn, reads=tuple(reads), dep_reads=extra,
                               writes=writes, inc=True)

    def dma(self, stream_name, out_ap, in_ap, deps=(), reads=(), writes=(),
            dep_reads=(), dep_writes=(), **kw):
        st = self.streams[stream_name]
        sem, cnt, queue = st
        deps = self._dep_list(deps, reads, writes, dep_reads, dep_writes)
        if cnt > 0:
            deps.append((sem, cnt))
        waits = self._waits(queue, deps)
        st[1] = cnt + 16
        tok = (sem, st[1])

        def fn(e, out_ap=out_ap, in_ap=in_ap, kw=kw):
            return e.dma_start(out=out_ap, in_=in_ap, **kw)

        self.ops[queue].append((waits, fn, (sem, 16)))
        for b in reads:
            b.rs.append(tok)
        for b in writes:
            b.w = tok
            b.rs = []
        return tok

    def emit(self):
        nc = self.nc
        with nc.Block() as block:
            def mk_body(eng_ops):
                def body(e):
                    for waits, fn, inc in eng_ops:
                        for sem, val in waits:
                            e.wait_ge(sem, val)
                        inst = fn(e)
                        if inc is not None:
                            inst.then_inc(inc[0], inc[1])
                return body
            block.tensor(mk_body(self.ops["pe"]))
            block.vector(mk_body(self.ops["dve"]))
            block.scalar(mk_body(self.ops["act"]))
            block.gpsimd(mk_body(self.ops["pool"]))
            block.sync(mk_body(self.ops["sp"]))


def build_nc(T=4096, CA=512, debug=False):
    CS = 2 * CA
    NSUP = T // CS
    NLTILE = CS // 128
    NWO = CA // 2
    NWOT = NWO // 128
    assert NWOT >= 1
    nc = bass.Bass("TRN2", target_bir_lowering=False, debug=debug)

    x_in = nc.dram_tensor("x_in", [T, H], F32, kind="ExternalInput").ap()
    wk_in = nc.dram_tensor("wk_in", [H, H], F32, kind="ExternalInput").ap()
    wv_in = nc.dram_tensor("wv_in", [H, H], F32, kind="ExternalInput").ap()
    wr_in = nc.dram_tensor("wr_in", [H, H], F32, kind="ExternalInput").ap()
    wo_in = nc.dram_tensor("wo_in", [H, H], F32, kind="ExternalInput").ap()
    wgu_in = nc.dram_tensor("wgu_in", [H, 2 * I_FF], F32, kind="ExternalInput").ap()
    wd_in = nc.dram_tensor("wd_in", [I_FF, H], F32, kind="ExternalInput").ap()
    td_in = nc.dram_tensor("td_in", [H], F32, kind="ExternalInput").ap()
    tf_in = nc.dram_tensor("tf_in", [H], F32, kind="ExternalInput").ap()
    tmk_in = nc.dram_tensor("tmk_in", [H], F32, kind="ExternalInput").ap()
    tmv_in = nc.dram_tensor("tmv_in", [H], F32, kind="ExternalInput").ap()
    tmr_in = nc.dram_tensor("tmr_in", [H], F32, kind="ExternalInput").ap()
    id_in = nc.dram_tensor("id_in", [128, 128], BF16, kind="ExternalInput").ap()
    y_out = nc.dram_tensor("y_out", [T, H], F32, kind="ExternalOutput").ap()

    wkv_s = nc.dram_tensor("wkv_s", [G, 2 * H, 128], BF16, kind="Internal").ap()
    wr2_s = nc.dram_tensor("wr2_s", [G // 2, 2 * H, 128], BF16, kind="Internal").ap()
    wo2_s = nc.dram_tensor("wo2_s", [G // 2, 2 * H, 128], BF16, kind="Internal").ap()
    wgu2_s = nc.dram_tensor("wgu2_s", [64, 2 * H, 128], BF16, kind="Internal").ap()
    wd_s = nc.dram_tensor("wd_s", [G, I_FF, 128], BF16, kind="Internal").ap()
    h_sp = nc.dram_tensor("h_sp", [64, 128, CS], BF16, kind="Internal").ap()
    rr_sp = nc.dram_tensor("rr_sp", [2 * NSUP, G, 128, CA], BF16, kind="Internal").ap()
    g_sp = nc.dram_tensor("g_sp", [G, 128, CS], BF16, kind="Internal").ap()

    A = nc.alloc_sbuf_tensor
    xnb = A("xnb", [128, G * (CS + 1)], BF16)
    mixk = A("mixk", [128, G * CA], BF16)
    ubuf = A("ubuf", [128, 32 * CA], BF16)
    hres1 = A("hres1", [128, 32 * CA], BF16)
    kbuf = A("kbuf", [128, 2 * CA], F32)
    vbuf = A("vbuf", [128, 2 * CA], F32)
    wbufs = A("wbufs", [128, CA], F32)
    wfbuf = A("wfbuf", [128, CA], F32)
    wfvbuf = A("wfvbuf", [128, CA], F32)
    ahist = A("ahist", [128, CA + 1], F32)
    bhist = A("bhist", [128, CA + 1], F32)
    tmpmix = A("tmpmix", [128, CA], F32)
    wslab = A("wslab", [128, 4 * H], BF16)
    wdslab = A("wdslab", [128, 2 * 4096], BF16)
    wstage = A("wstage", [128, 2 * H], BF16)
    xtile = A("xtile", [128, 2 * H], F32)
    xna = A("xna", [128, 2 * H], BF16)
    histage = A("histage", [128, CA], BF16)
    gstage = A("gstage", [128, CA], BF16)
    rrstage = A("rrstage", [128, 2 * CA], BF16)
    gtile = A("gtile", [128, 2 * NWO], BF16)
    pstage = A("pstage", [128, 2 * NWO], BF16)
    tmpg = A("tmpg", [128, CA], F32)
    statsb = A("statsb", [128, 2 * 24], F32)
    mvb = A("mvb", [128, 2 * 2], F32)
    lnt = A("lnt", [128, 2], F32)
    rstdb = A("rstdb", [128, 2], F32)
    tdr = A("tdr", [128, G], F32)
    etd = A("etd", [128, G], F32)
    dcol = A("dcol", [128, G], F32)
    ctd = A("ctd", [128, G], F32)
    tfc = A("tfc", [128, G], F32)
    tmkc = A("tmkc", [128, G], F32)
    tmvc = A("tmvc", [128, G], F32)
    tmrc = A("tmrc", [128, G], F32)
    mprev = A("mprev", [128, G], F32)
    acar = A("acar", [128, G], F32)
    bcar = A("bcar", [128, G], F32)
    ident = A("ident", [128, 128], BF16)
    rmx = A("rmx", [128, 1], F32)
    mnew = A("mnew", [128, 1], F32)
    dmb = A("dmb", [128, 1], F32)
    rescb = A("rescb", [128, 1], F32)
    negm = A("negm", [128, 1], F32)
    biasf = A("biasf", [128, 1], F32)

    PSM = [nc.alloc_psum_tensor("psm%d" % i, [128, 512], F32) for i in range(6)]
    PST = [nc.alloc_psum_tensor("pst%d" % i, [128, 1024], BF16) for i in range(2)]

    P = Prog(nc)
    for _i in range(6):
        P.stream("w%d" % _i, "sp")
    for _i in range(4):
        P.stream("hr%d" % _i, "sp")
    for _i in range(4):
        P.stream("wd%d" % _i, "sp")
        P.stream("x%d" % _i, "sp")
        P.stream("out%d" % _i, "sp")
        P.stream("swp%d" % _i, "act")
        P.stream("swh%d" % _i, "act")
        P.stream("swrr%d" % _i, "act")
        P.stream("swg%d" % _i, "act")
        P.stream("pro%d" % _i, "pool")
        P.stream("swq%d" % _i, "pool")
        P.stream("prr%d" % _i, "pool")
    P.stream("cst", "sp")
    P.stream("rrn", "sp")
    P.stream("gt0", "sp")
    P.stream("gt1", "sp")

    b_xnb = Buf()
    b_mixk = Buf()
    b_mixv = Buf()
    b_rsig = Buf()
    b_u_h = [Buf() for _ in range(32)]
    b_hres = [Buf() for _ in range(32)]
    b_kbuf = [Buf(), Buf()]
    b_vbuf = [Buf(), Buf()]
    b_scan = Buf()
    b_tmpmix = Buf()
    b_wslab = [Buf() for _ in range(4)]
    b_wdslab = [Buf() for _ in range(4)]
    b_wstage = [Buf(), Buf()]
    b_xtile = [Buf(), Buf()]
    b_xna = [Buf(), Buf()]
    b_histage = [Buf()]
    b_gstage = [Buf()]
    b_rrstage = [Buf(), Buf()]
    b_gtile = [Buf(), Buf()]
    b_pstage = [Buf(), Buf()]
    b_tmpg = [Buf()]
    b_stats = [Buf(), Buf()]
    b_carry = Buf()
    b_psm = [Buf() for _ in range(6)]
    b_hsp = [Buf() for _ in range(64)]
    b_rrsp = [[Buf() for _ in range(G)] for _ in range(2 * NSUP)]
    b_gsp = [[Buf(), Buf()] for _ in range(G)]
    b_rrn = [Buf(), Buf()]
    b_pst = [Buf(), Buf()]
    b_consts = Buf()

    xnb_g = xnb[:].rearrange("p (g c) -> p g c", g=G)

    def ldtbl(set_id):
        def fn(e, set_id=set_id):
            return e.add_instruction(mybir.InstLoadActFuncSet(
                name=nc.get_next_instruction_name(),
                act_func_set_id=set_id, ins=[], outs=[]))
        return P.op("act", fn, inc=False)

    for vec_in, dst in ((td_in, tdr), (tf_in, tfc), (tmk_in, tmkc),
                       (tmv_in, tmvc), (tmr_in, tmrc)):
        for g in range(G):
            P.dma("cst", dst[:, g:g + 1], vec_in[g * 128:(g + 1) * 128], writes=(b_consts,))
    P.dma("cst", ident[:], id_in[:], writes=(b_consts,))

    ldtbl(TBL_LN_EXP)
    P.op("act", lambda e: e.activation(etd[:], tdr[:], AF.Exp), reads=(b_consts,), writes=(b_consts,))
    P.op("act", lambda e: e.activation(dcol[:], etd[:], AF.Exp, scale=-1.0), reads=(b_consts,), writes=(b_consts,))
    P.op("dve", lambda e: e.tensor_single_scalar(ctd[:], etd[:], float(-CA), ALU.mult), reads=(b_consts,), writes=(b_consts,))
    P.op("dve", lambda e: e.memset(mprev[:], -1e30), writes=(b_carry,))
    P.op("dve", lambda e: e.memset(acar[:], 0.0), writes=(b_carry,))
    P.op("dve", lambda e: e.memset(bcar[:], 0.0), writes=(b_carry,))
    P.op("dve", lambda e: e.memset(xnb_g[:, :, 0:1], 0.0), writes=(b_xnb,))

    wtoks = {}

    def cast_weight(name, src_ap, dst_ap, nslab, col0_fn, dst_idx_fn):
        toks = []
        for j in range(nslab):
            par = j % 2
            stg3 = wstage[:, par * H:par * H + H].rearrange("p (kb j) -> p kb j", j=128)
            r0, c0 = col0_fn(j)
            src3 = src_ap[r0:r0 + 2048, c0:c0 + 128].rearrange("(kb p) j -> p kb j", p=128)
            P.dma("pro%d" % par, stg3, src3, writes=(b_wstage[par],))
            d0, d1 = dst_idx_fn(j)
            dst3 = dst_ap[d0, d1:d1 + 2048, :].rearrange("(kb p) j -> p kb j", p=128)
            t = P.dma("swq%d" % par, dst3, stg3, reads=(b_wstage[par],))
            toks.append(t)
        wtoks[name] = toks

    cast_weight("wr", wr_in, wr2_s, G, lambda j: (0, j * 128),
                lambda j: (j // 2, (j % 2) * 2048))
    cast_weight("wk", wk_in, wkv_s, G, lambda j: (0, j * 128), lambda j: (j, 0))
    cast_weight("wv", wv_in, wkv_s, G, lambda j: (0, j * 128), lambda j: (j, 2048))
    cast_weight("wgu", wgu_in, wgu2_s, 128, lambda j: (0, j * 128),
                lambda j: (j, 0) if j < 64 else (j - 64, 2048))
    cast_weight("wd", wd_in, wd_s, G * 4,
                lambda j: ((j % 4) * 2048, (j // 4) * 128),
                lambda j: (j // 4, (j % 4) * 2048))
    cast_weight("wo", wo_in, wo2_s, G, lambda j: (0, j * 128),
                lambda j: (j // 2, (j % 2) * 2048))

    def load_pair(scr_ap, toks, row, par, stream):
        dst = wslab[:, par * 4096:(par + 1) * 4096].rearrange("p (kb j) -> p kb j", j=128)
        srcv = scr_ap[row].rearrange("(kb p) j -> p kb j", p=128)
        deps = [wtoks[n][i] for n, i in toks]
        return P.dma(stream, dst, srcv, deps=deps,
                     writes=(b_wslab[2 * par], b_wslab[2 * par + 1]))

    def load_wd_q(g_out, q):
        slot = (g_out + q) % 4
        dst = wdslab[:, slot * 2048:(slot + 1) * 2048].rearrange("p (i j) -> p i j", j=128)
        srcv = wd_s[g_out, q * 2048:(q + 1) * 2048, :].rearrange("(i p) j -> p i j", p=128)
        return P.dma("wd%d" % slot, dst, srcv, deps=[wtoks["wd"][g_out * 4 + q]],
                     writes=(b_wdslab[slot],))

    def wsl2(par, sub, kb):
        return wslab[:, par * 4096 + sub * 2048 + kb * 128: par * 4096 + sub * 2048 + (kb + 1) * 128]

    for s in range(NSUP):
        VE = "dve" if s == 0 else "pool"
        ldtbl(TBL_LN_EXP)
        for r in range(NLTILE):
            par = r % 2
            xt = xtile[:, par * H:(par + 1) * H]
            t0 = s * CS + r * 128
            P.dma("x%d" % par, xt, x_in[t0:t0 + 128, :], writes=(b_xtile[par],))
            st = statsb[:, par * 24:(par + 1) * 24]
            for q in range(4):
                P.op("dve", lambda e, st=st, q=q, xt=xt: e.bn_stats(
                    st[:, 6 * q:6 * q + 6], xt[:, 512 * q:512 * (q + 1)]),
                    reads=(b_xtile[par],), writes=(b_stats[par],))
            mv = mvb[:, par * 2:(par + 1) * 2]
            P.op("dve", lambda e, mv=mv, st=st: e.bn_aggr(mv, st),
                 reads=(b_stats[par],), writes=(b_stats[par],))
            ln_c = lnt[:, par:par + 1]
            P.op("dve", lambda e, ln_c=ln_c, mv=mv: e.tensor_single_scalar(ln_c, mv[:, 1:2], 1e-5, ALU.add),
                 reads=(b_stats[par],), writes=(b_stats[par],))
            P.op("act", lambda e, ln_c=ln_c: e.activation(ln_c, ln_c, AF.Ln),
                 reads=(b_stats[par],), writes=(b_stats[par],))
            rs_c = rstdb[:, par:par + 1]
            P.op("act", lambda e, rs_c=rs_c, ln_c=ln_c: e.activation(rs_c, ln_c, AF.Exp, scale=-0.5),
                 reads=(b_stats[par],), writes=(b_stats[par],))
            xa = xna[:, par * H:(par + 1) * H]
            P.op("dve", lambda e, xa=xa, xt=xt, mv=mv, rs_c=rs_c: e.tensor_scalar(
                xa, xt, mv[:, 0:1], rs_c, ALU.subtract, ALU.mult),
                reads=(b_xtile[par], b_stats[par]), writes=(b_xna[par],))
            for half in range(2):
                pidx = (r * 2 + half) % 2
                items = []
                for gg in range(8):
                    g = half * 8 + gg

                    def tfn(e, g=g, gg=gg, xa=xa, pidx=pidx):
                        return e.matmul(PST[pidx][:, gg * 128:(gg + 1) * 128],
                                        xa[:, g * 128:(g + 1) * 128], ident[:],
                                        is_transpose=True, start=True, stop=True,
                                        skip_group_check=True)
                    items.append((tfn, ()))
                P.pe_group(items, reads=(b_xna[par], b_consts), writes=(b_pst[pidx],))
                dstv = xnb_g[:, half * 8:half * 8 + 8, 1 + r * 128:1 + r * 128 + 128]
                srcv = PST[pidx][:].rearrange("p (g j) -> p g j", j=128)
                P.op("dve", lambda e, dstv=dstv, srcv=srcv: e.tensor_copy(dstv, srcv),
                     reads=(b_pst[pidx],), writes=(b_xnb,))

        for X in range(2):
            c_lo = 1 + X * CA
            chunk_id = 2 * s + X

            def emit_mix(dst_ap, tmc, bdst):
                for kb in range(G):
                    cur = xnb_g[:, kb, c_lo:c_lo + CA]
                    sh = xnb_g[:, kb, c_lo - 1:c_lo - 1 + CA]
                    P.op("dve", lambda e, cur=cur, sh=sh: e.tensor_sub(tmpmix[:], cur, sh),
                         reads=(b_xnb,), writes=(b_tmpmix,))
                    dsl = dst_ap[:, kb * CA:(kb + 1) * CA]
                    xdep = (b_rrn[0], b_rrn[1]) if bdst is b_mixk else (b_gtile[0], b_gtile[1])
                    P.op("dve", lambda e, dsl=dsl, tmc=tmc, kb=kb, sh=sh: e.scalar_tensor_tensor(
                        dsl, tmpmix[:], tmc[:, kb:kb + 1], sh, ALU.mult, ALU.add),
                        reads=(b_tmpmix, b_consts, b_xnb), writes=(bdst,), dep_writes=xdep)

            ldtbl(TBL_SIGMOID)
            emit_mix(mixk[:], tmrc, b_mixk)
            for g in range(G):
                par = (g // 2) % 2
                sub = g % 2
                if sub == 0:
                    load_pair(wr2_s, [("wr", g), ("wr", g + 1)], g // 2, par, "w%d" % (2 + par))
                pid = 4 + (g % 2)
                items = []
                for kb in range(G):
                    def mf(e, kb=kb, par=par, sub=sub, pid=pid):
                        return e.matmul(PSM[pid][:, 0:CA], wsl2(par, sub, kb),
                                        mixk[:, kb * CA:(kb + 1) * CA],
                                        start=(kb == 0), stop=(kb == G - 1))
                    items.append((mf, ()))
                P.pe_group(items, reads=(b_wslab[2 * par + sub], b_mixk), writes=(b_psm[pid],))
                rsl = ubuf[:, G * CA + g * CA: G * CA + (g + 1) * CA]
                P.op("act", lambda e, rsl=rsl, pid=pid: e.activation(rsl, PSM[pid][:, 0:CA], AF.Sigmoid),
                     reads=(b_psm[pid],), writes=(b_rsig,))

            ldtbl(TBL_LN_EXP)
            emit_mix(mixk[:], tmkc, b_mixk)
            emit_mix(ubuf[:, 0:G * CA], tmvc, b_mixv)
            for g in range(G):
                par = g % 2
                gc = slice(g, g + 1)
                load_pair(wkv_s, [("wk", g), ("wv", g)], g, par, "w%d" % par)
                items = []
                for kb in range(G):
                    def mf(e, kb=kb, par=par):
                        return e.matmul(PSM[par][:, 0:CA], wsl2(par, 0, kb),
                                        mixk[:, kb * CA:(kb + 1) * CA],
                                        start=(kb == 0), stop=(kb == G - 1))
                    items.append((mf, ()))
                P.pe_group(items, reads=(b_wslab[2 * par], b_mixk), writes=(b_psm[par],))
                kg = kbuf[:, par * CA:(par + 1) * CA]
                P.op("act", lambda e, kg=kg, par=par: e.copy(kg, PSM[par][:, 0:CA]),
                     reads=(b_psm[par],), writes=(b_kbuf[par],))
                items = []
                for kb in range(G):
                    def mf(e, kb=kb, par=par):
                        return e.matmul(PSM[2 + par][:, 0:CA], wsl2(par, 1, kb),
                                        ubuf[:, kb * CA:(kb + 1) * CA],
                                        start=(kb == 0), stop=(kb == G - 1))
                    items.append((mf, ()))
                P.pe_group(items, reads=(b_wslab[2 * par + 1], b_mixv), writes=(b_psm[2 + par],))
                vg = vbuf[:, par * CA:(par + 1) * CA]
                P.op("act", lambda e, vg=vg, par=par: e.copy(vg, PSM[2 + par][:, 0:CA]),
                     reads=(b_psm[2 + par],), writes=(b_vbuf[par],))

                P.op("dve", lambda e, kg=kg: e.reduce_max(rmx[:], kg, axis=AX.X),
                     reads=(b_kbuf[par],), writes=(b_scan,))
                P.op(VE, lambda e, gc=gc: e.tensor_scalar(mnew[:], mprev[:, gc], ctd[:, gc], rmx[:],
                                                          ALU.add, ALU.max),
                     reads=(b_carry, b_consts, b_scan), writes=(b_scan,))
                P.op(VE, lambda e: e.tensor_single_scalar(negm[:], mnew[:], -1.0, ALU.mult),
                     reads=(b_scan,), writes=(b_scan,))
                P.op("act", lambda e, gc=gc: e.activation(rescb[:], mprev[:, gc], AF.Exp, bias=negm[:]),
                     reads=(b_scan, b_carry), writes=(b_scan,))
                P.op(VE, lambda e, gc=gc: e.tensor_mul(acar[:, gc], acar[:, gc], rescb[:]),
                     reads=(b_scan,), writes=(b_carry,))
                P.op(VE, lambda e, gc=gc: e.tensor_mul(bcar[:, gc], bcar[:, gc], rescb[:]),
                     reads=(b_scan,), writes=(b_carry,))
                P.op(VE, lambda e, gc=gc: e.tensor_copy(mprev[:, gc], mnew[:]),
                     reads=(b_scan,), writes=(b_carry,))
                P.op(VE, lambda e, gc=gc: e.tensor_scalar(biasf[:], tfc[:, gc], negm[:], None, ALU.add),
                     reads=(b_scan, b_consts), writes=(b_scan,))
                P.op("act", lambda e, kg=kg: e.activation(wbufs[:], kg, AF.Exp, bias=negm[:]),
                     reads=(b_kbuf[par], b_scan), writes=(b_scan,))
                P.op("act", lambda e, kg=kg: e.activation(wfbuf[:], kg, AF.Exp, bias=biasf[:]),
                     reads=(b_kbuf[par], b_scan), writes=(b_scan,))
                P.op(VE, lambda e, kg=kg, vg=vg: e.tensor_mul(kg, wbufs[:], vg),
                     reads=(b_scan, b_vbuf[par]), writes=(b_kbuf[par],))
                P.op(VE, lambda e, vg=vg: e.tensor_mul(wfvbuf[:], wfbuf[:], vg),
                     reads=(b_scan, b_vbuf[par]), writes=(b_scan,))
                dbc = dcol[:, g:g + 1].broadcast_to((128, CA))
                P.op("dve", lambda e, gc=gc, kg=kg, dbc=dbc: e.tensor_tensor_scan(
                    ahist[:, 1:CA + 1], dbc, kg, acar[:, gc], ALU.mult, ALU.add),
                    reads=(b_kbuf[par], b_carry, b_consts), writes=(b_scan,))
                P.op(VE, lambda e, gc=gc: e.tensor_copy(ahist[:, 0:1], acar[:, gc]),
                     reads=(b_carry,), writes=(b_scan,))
                P.op("dve", lambda e, gc=gc, dbc=dbc: e.tensor_tensor_scan(
                    bhist[:, 1:CA + 1], dbc, wbufs[:], bcar[:, gc], ALU.mult, ALU.add),
                    reads=(b_scan, b_carry, b_consts), writes=(b_scan,))
                P.op(VE, lambda e, gc=gc: e.tensor_copy(bhist[:, 0:1], bcar[:, gc]),
                     reads=(b_carry,), writes=(b_scan,))
                P.op(VE, lambda e, gc=gc: e.tensor_copy(acar[:, gc], ahist[:, CA:CA + 1]),
                     reads=(b_scan,), writes=(b_carry,))
                P.op(VE, lambda e, gc=gc: e.tensor_copy(bcar[:, gc], bhist[:, CA:CA + 1]),
                     reads=(b_scan,), writes=(b_carry,))
                P.op(VE, lambda e: e.tensor_add(wfvbuf[:], ahist[:, 0:CA], wfvbuf[:]),
                     reads=(b_scan,), writes=(b_scan,))
                P.op(VE, lambda e: e.tensor_add(wfbuf[:], bhist[:, 0:CA], wfbuf[:]),
                     reads=(b_scan,), writes=(b_scan,))
                P.op("dve", lambda e: e.reciprocal(wfbuf[:], wfbuf[:]),
                     reads=(b_scan,), writes=(b_scan,))
                P.op(VE, lambda e: e.tensor_mul(wfvbuf[:], wfvbuf[:], wfbuf[:]),
                     reads=(b_scan,), writes=(b_scan,))
                rsl = ubuf[:, G * CA + g * CA: G * CA + (g + 1) * CA]
                rrd = rrstage[:, par * CA:(par + 1) * CA]
                P.op(VE, lambda e, rrd=rrd, rsl=rsl: e.tensor_mul(rrd, wfvbuf[:], rsl),
                     reads=(b_scan, b_rsig), writes=(b_rrstage[par],))
                P.dma(("swrr%d" if s == 0 else "prr%d") % par, rr_sp[chunk_id, g], rrd,
                      reads=(b_rrstage[par],), writes=(b_rrsp[chunk_id][g],))

        ldtbl(TBL_SILU)
        for i in range(64):
            setg = (2 * i) % 3
            setu = (2 * i + 1) % 3
            parw = i % 2
            load_pair(wgu2_s, [("wgu", i), ("wgu", 64 + i)], i, parw, "w%d" % parw)
            for n in range(2):
                pid = 2 * setg + n
                items = []
                for kb in range(G):
                    def mf(e, kb=kb, n=n, parw=parw, pid=pid):
                        return e.matmul(PSM[pid][:, 0:CA], wsl2(parw, 0, kb),
                                        xnb_g[:, kb, 1 + n * CA:1 + (n + 1) * CA],
                                        start=(kb == 0), stop=(kb == G - 1))
                    items.append((mf, ()))
                P.pe_group(items, reads=(b_wslab[2 * parw], b_xnb), writes=(b_psm[pid],))
            for n in range(2):
                pid = 2 * setu + n
                items = []
                for kb in range(G):
                    def mf(e, kb=kb, n=n, parw=parw, pid=pid):
                        return e.matmul(PSM[pid][:, 0:CA], wsl2(parw, 1, kb),
                                        xnb_g[:, kb, 1 + n * CA:1 + (n + 1) * CA],
                                        start=(kb == 0), stop=(kb == G - 1))
                    items.append((mf, ()))
                P.pe_group(items, reads=(b_wslab[2 * parw + 1], b_xnb), writes=(b_psm[pid],))
            part = i % 2
            for n in range(2):
                pg = 2 * setg + n
                pu = 2 * setu + n
                tg = tmpg[:, 0:CA]
                P.op("act", lambda e, tg=tg, pg=pg: e.activation(tg, PSM[pg][:, 0:CA], AF.Silu),
                     reads=(b_psm[pg],), writes=(b_tmpg[0],))
                if n == 0:
                    if i < 32:
                        hdst = hres1[:, i * CA:(i + 1) * CA]
                        hb = b_hres[i]
                        hxd = ()
                    else:
                        hdst = ubuf[:, (i - 32) * CA:(i - 31) * CA]
                        hb = b_u_h[i - 32]
                        j = i - 32
                        hxd = ((b_mixv, b_gtile[0], b_gtile[1]) if j < 16 else (b_rsig,))
                    P.op("dve", lambda e, hdst=hdst, tg=tg, pu=pu: e.tensor_mul(hdst, tg, PSM[pu][:, 0:CA]),
                         reads=(b_tmpg[0], b_psm[pu]), writes=(hb,), dep_writes=hxd)
                else:
                    hdst = histage[:, 0:CA]
                    P.op("dve", lambda e, hdst=hdst, tg=tg, pu=pu: e.tensor_mul(hdst, tg, PSM[pu][:, 0:CA]),
                         reads=(b_tmpg[0], b_psm[pu]), writes=(b_histage[0],))
                    P.dma("swh%d" % part, h_sp[i, :, CA:CS], hdst, reads=(b_histage[0],), writes=(b_hsp[i],))

        for half in range(2):
            if half == 1:
                for blk in range(4):
                    i0 = blk * 16
                    if i0 < 32:
                        dst = hres1[:, i0 * CA:(i0 + 16) * CA].rearrange("p (i c) -> p i c", c=CA)
                        wb = [b_hres[i0 + j] for j in range(16)]
                    else:
                        dst = ubuf[:, (i0 - 32) * CA:(i0 - 16) * CA].rearrange("p (i c) -> p i c", c=CA)
                        wb = [b_u_h[i0 - 32 + j] for j in range(16)]
                    src = h_sp[i0:i0 + 16, :, CA:CS].rearrange("i p c -> p i c")
                    P.dma("hr%d" % blk, dst, src, writes=wb,
                          reads=tuple(b_hsp[i0:i0 + 16]))
            for g_out in range(G):
                pid = g_out % 2
                for q in range(4):
                    load_wd_q(g_out, q)
                items = []
                hread = []
                for i in range(64):
                    hb = b_hres[i] if i < 32 else b_u_h[i - 32]
                    hread.append(hb)
                    hsrc = (hres1[:, i * CA:(i + 1) * CA] if i < 32
                            else ubuf[:, (i - 32) * CA:(i - 31) * CA])
                    slot = (g_out + i // 16) % 4
                    half_sl = wdslab[:, slot * 2048 + (i % 16) * 128:
                                     slot * 2048 + (i % 16 + 1) * 128]

                    def mf(e, half_sl=half_sl, hsrc=hsrc, i=i, pid=pid):
                        return e.matmul(PSM[pid][:, 0:CA], half_sl, hsrc,
                                        start=(i == 0), stop=(i == 63))
                    ext = [b_wdslab[slot]] if i % 16 == 0 else []
                    items.append((mf, tuple(ext + [hb])))
                P.pe_group(items, reads=tuple(set(hread)) + tuple(b_wdslab),
                           writes=(b_psm[pid],), first_deps=())
                part = g_out % 2
                gdst = gstage[:, 0:CA]
                P.op("act", lambda e, gdst=gdst, pid=pid: e.activation(gdst, PSM[pid][:, 0:CA], AF.Silu),
                     reads=(b_psm[pid],), writes=(b_gstage[0],))
                P.dma("swg%d" % part, g_sp[g_out, :, half * CA:(half + 1) * CA], gdst, reads=(b_gstage[0],), writes=(b_gsp[g_out][half],))

        for n in range(CS // NWO):
            chunk_id = 2 * s + (n * NWO) // CA
            nn = (n * NWO) % CA // NWO
            par_n = n % 2
            rrn = mixk[:, par_n * G * NWO:(par_n + 1) * G * NWO].rearrange("p (g c) -> p g c", c=NWO)
            rsrc = rr_sp[chunk_id, :, :, nn * NWO:(nn + 1) * NWO].rearrange("g p c -> p g c")
            P.dma("rrn", rrn, rsrc, writes=(b_rrn[par_n],), reads=tuple(b_rrsp[chunk_id]))
            gdst_blk = ubuf[:, par_n * G * NWO:(par_n + 1) * G * NWO].rearrange("p (g c) -> p g c", c=NWO)
            gsrc = g_sp[:, :, n * NWO:(n + 1) * NWO].rearrange("g p c -> p g c")
            gdeps = [b.w for gb in b_gsp for b in gb if b.w is not None]
            P.dma("gt%d" % par_n, gdst_blk, gsrc, deps=gdeps,
                  dep_writes=tuple(b_u_h[par_n * 8:(par_n + 1) * 8]),
                  writes=(b_gtile[par_n],))
            t0 = s * CS + n * NWO
            for tt in range(NWOT):
                par = tt % 2
                xt = xtile[:, par * H:(par + 1) * H]
                P.dma("x%d" % par, xt, x_in[t0 + tt * 128:t0 + tt * 128 + 128, :], writes=(b_xtile[par],))
            for g_out in range(G):
                sl4 = g_out % 4
                pid = g_out % 2
                dst = wslab[:, sl4 * 2048:(sl4 + 1) * 2048].rearrange("p (kb j) -> p kb j", j=128)
                wsrc = wo2_s[g_out // 2, (g_out % 2) * 2048:(g_out % 2 + 1) * 2048, :].rearrange(
                    "(kb p) j -> p kb j", p=128)
                P.dma("w%d" % sl4, dst, wsrc, deps=[wtoks["wo"][g_out]], writes=(b_wslab[sl4],))
                items = []
                for kb in range(G):
                    def mf(e, kb=kb, sl4=sl4, pid=pid, par_n=par_n):
                        return e.matmul(PSM[pid][:, 0:NWO], wsl2(sl4 // 2, sl4 % 2, kb),
                                        mixk[:, par_n * G * NWO + kb * NWO:par_n * G * NWO + (kb + 1) * NWO],
                                        start=(kb == 0), stop=(kb == G - 1))
                    items.append((mf, ()))
                P.pe_group(items, reads=(b_wslab[sl4], b_rrn[par_n]), writes=(b_psm[pid],))
                gt = ubuf[:, par_n * G * NWO + g_out * NWO: par_n * G * NWO + (g_out + 1) * NWO]
                parg = g_out % 2
                ps = pstage[:, parg * NWO:(parg + 1) * NWO]
                P.op("dve", lambda e, ps=ps, gt=gt, pid=pid: e.tensor_mul(ps, PSM[pid][:, 0:NWO], gt),
                     reads=(b_psm[pid], b_gtile[par_n]), writes=(b_pstage[parg],))
                for tt in range(NWOT):
                    pt = (g_out * NWOT + tt) % 2

                    def tfn(e, ps=ps, tt=tt, pt=pt, g_out=g_out):
                        return e.matmul(PST[pt][:, (g_out % 8) * 128:(g_out % 8 + 1) * 128],
                                        ps[:, tt * 128:(tt + 1) * 128], ident[:],
                                        is_transpose=True, start=True, stop=True,
                                        skip_group_check=True)
                    P.pe_group([(tfn, ())], reads=(b_pstage[parg], b_consts), writes=(b_pst[pt],))
                    xt = xtile[:, (tt % 2) * H:(tt % 2 + 1) * H]
                    P.op("dve", lambda e, xt=xt, pt=pt, g_out=g_out: e.tensor_add(
                        xt[:, g_out * 128:(g_out + 1) * 128], xt[:, g_out * 128:(g_out + 1) * 128],
                        PST[pt][:, (g_out % 8) * 128:(g_out % 8 + 1) * 128]),
                        reads=(b_pst[pt],), writes=(b_xtile[tt % 2],))
            for tt in range(NWOT):
                par = tt % 2
                xt = xtile[:, par * H:(par + 1) * H]
                P.dma("out%d" % par, y_out[t0 + tt * 128:t0 + tt * 128 + 128, :], xt, reads=(b_xtile[par],))

        if s + 1 < NSUP:
            P.op("dve", lambda e: e.tensor_copy(xnb_g[:, :, 0:1], xnb_g[:, :, CS:CS + 1]),
                 reads=(b_xnb,), writes=(b_xnb,))

    fin_waits = [(P.streams[s][0], P.streams[s][1]) for s in ("out0", "out1") if P.streams[s][1] > 0]
    P.ops["sp"].append((fin_waits, lambda e: e.nop(), None))

    P.emit()
    nc.finalize()
    return nc


_NC_CACHE = {}


def kernel(L0_attn_hidden, L0_attn_time_decay, L0_attn_time_first,
           L0_attn_time_mix_key, L0_attn_time_mix_value, L0_attn_time_mix_receptance,
           L0_attn_Wk, L0_attn_Wv, L0_attn_Wr, L0_attn_Wo,
           L0_ffn_w_gate_up, L0_ffn_w_down):
    x = np.ascontiguousarray(np.asarray(L0_attn_hidden, dtype=np.float32))
    B, T, H_ = x.shape
    assert (B, H_) == (NCORE, H)
    wk = np.ascontiguousarray(np.asarray(L0_attn_Wk, dtype=np.float32))
    wv = np.ascontiguousarray(np.asarray(L0_attn_Wv, dtype=np.float32))
    wr = np.ascontiguousarray(np.asarray(L0_attn_Wr, dtype=np.float32))
    wo = np.ascontiguousarray(np.asarray(L0_attn_Wo, dtype=np.float32))
    wgu = np.ascontiguousarray(np.asarray(L0_ffn_w_gate_up, dtype=np.float32).T)
    wd = np.ascontiguousarray(np.asarray(L0_ffn_w_down, dtype=np.float32).T)
    td = np.ascontiguousarray(np.asarray(L0_attn_time_decay, dtype=np.float32).reshape(H))
    tf = np.ascontiguousarray(np.asarray(L0_attn_time_first, dtype=np.float32).reshape(H))
    tmk = np.ascontiguousarray(np.asarray(L0_attn_time_mix_key, dtype=np.float32).reshape(H))
    tmv = np.ascontiguousarray(np.asarray(L0_attn_time_mix_value, dtype=np.float32).reshape(H))
    tmr = np.ascontiguousarray(np.asarray(L0_attn_time_mix_receptance, dtype=np.float32).reshape(H))
    ident = np.eye(128, dtype=ml_dtypes.bfloat16)

    key = (T,)
    if key not in _NC_CACHE:
        _NC_CACHE[key] = build_nc(T=T)
    nc = _NC_CACHE[key]

    in_maps = []
    for b in range(NCORE):
        in_maps.append({
            "x_in": x[b],
            "wk_in": wk, "wv_in": wv, "wr_in": wr, "wo_in": wo,
            "wgu_in": wgu, "wd_in": wd,
            "td_in": td, "tf_in": tf,
            "tmk_in": tmk, "tmv_in": tmv, "tmr_in": tmr,
            "id_in": ident,
        })
    res = run_bass_kernel_spmd(nc, in_maps, core_ids=list(range(NCORE)))
    out = np.stack([res.results[b]["y_out"] for b in range(NCORE)], axis=0)
    return out.astype(np.float32)
